# version 01
# baseline (speedup 1.0000x reference)
import contextlib
import math
import numpy as np
import ml_dtypes
import concourse.bass as bass
import concourse.mybir as mybir
from concourse.bass_utils import run_bass_kernel_spmd

F32 = mybir.dt.float32
I32 = mybir.dt.int32
BF16 = mybir.dt.bfloat16
FP8 = mybir.dt.float8e4
AOP = mybir.AluOpType
AFT = mybir.ActivationFunctionType
AXL = mybir.AxisListType

CMAGIC = 3.0 * 2.0**22
EXP_MASK = 0x7F800000
E23 = float(1 << 23)

S = 2048
H = 2048
II = 8192
HEADS, G, DH = 16, 4, 128
SCALE = DH ** -0.5
NT = S // 128
NH = H // 128
NO = II // 512
NOC = II // 128


def f_e(e):
    return float(e) * E23


class Plan:
    ENGS = ("v", "s", "t", "sync", "gp")

    def __init__(self):
        self.steps = {e: [] for e in self.ENGS}
        self.n = {e: 0 for e in ("v", "s", "t")}
        self.nd = {e: 0 for e in ("sync", "gp")}

    def op(self, eng, fn, inc=None, waits=()):
        auto = eng in ("v", "s")
        do_inc = auto if inc is None else inc
        nb = self.n[eng]
        wl = [(k, v) for (k, v) in waits if v > 0]

        def step(e, sems):
            for k, val in wl:
                e.wait_ge(sems[k], val)
            instr = fn(e)
            if auto and nb > 0:
                instr._wait_ge(sems[eng], nb)
            if do_inc:
                instr.then_inc(sems[eng], 1)
        self.steps[eng].append(step)
        if do_inc:
            self.n[eng] += 1
        return self.n[eng]

    def wait(self, eng, sem_key, value):
        if value <= 0:
            return
        def step(e, sems):
            e.wait_ge(sems[sem_key], value)
        self.steps[eng].append(step)

    def dma(self, q, fn, waits=()):
        wl = [(k, v) for (k, v) in waits if v > 0]

        def step(e, sems):
            for k, val in wl:
                e.wait_ge(sems[k], val)
            fn(e).then_inc(sems[q], 16)
        self.steps[q].append(step)
        self.nd[q] += 16
        return self.nd[q]

    def barrier(self):
        marks = dict(self.n)
        dmarks = dict(self.nd)
        for eng in self.ENGS:
            for k, v in marks.items():
                if k != eng:
                    self.wait(eng, k, v)
            for k, v in dmarks.items():
                if k != eng:
                    self.wait(eng, k, v)


class MXQuant:

    def __init__(self, ctx, nc, name, p, fmax, side=None):
        self.p, self.fmax = p, fmax
        nb = fmax // 32
        kw = {"side": side} if side else {}
        sb = lambda n, shp, dt: ctx.enter_context(
            nc.sbuf_tensor(f"{name}_{n}", shp, dt, **kw))
        self.am = sb("am", [p, nb], F32)
        self.t = sb("t", [p, nb], F32)
        self.b2 = sb("b2", [p, nb], F32)
        self.madj = sb("madj", [p, nb], F32)
        self.ae_i = sb("ae", [p, nb], I32)
        self.es_i = sb("es", [p, nb], I32)
        self.tsc_i = sb("tsc", [p, nb], I32)
        self.ers_i = sb("ers", [p, nb], I32)
        self.sa = sb("sa", [p, fmax], F32)
        self.sc = sb("sc", [p, fmax], F32)
        self.se = sb("se", [p, fmax], F32)
        self.sb_i = sb("sbi", [p, fmax], I32)
        self.sd_i = sb("sdi", [p, fmax], I32)

    def emit(self, P, x_ap, out_ap, mfrac, max_norm, act, p=None, f=None,
             eng="v", waits=()):
        p = p if p is not None else self.p
        f = f if f is not None else self.fmax
        nb = f // 32
        K_SN = -float(mfrac) * E23 - 2147483648.0
        x3 = x_ap.rearrange("p (n b) -> p n b", b=32)

        def B(t):
            return t[:p, :nb]

        def BB(t):
            return t[:p, :nb].unsqueeze(2).broadcast_to([p, nb, 32])

        def E(t):
            return t[:p, :f]

        def E3(t):
            return t[:p, :f].rearrange("p (n b) -> p n b", b=32)

        O = lambda fn, w=(): P.op(eng, fn, waits=w)
        O(lambda e: e.tensor_reduce(B(self.am), x3, AXL.X, AOP.max,
                                    apply_absolute_value=True), w=waits)
        O(lambda e: e.tensor_scalar(B(self.am), B(self.am), 2.0**-98, 2.0**98,
                                    AOP.max, AOP.min))
        O(lambda e: e.tensor_scalar(B(self.ae_i), B(self.am).bitcast(I32),
                                    EXP_MASK, None, AOP.bitwise_and))
        O(lambda e: e.tensor_scalar(B(self.es_i), B(self.ae_i), -f_e(2), None,
                                    AOP.add))
        O(lambda e: e.tensor_scalar(B(self.ers_i), B(self.ae_i), -1.0,
                                    2147483648.0, AOP.mult, AOP.add))
        if act:
            O(lambda e: e.tensor_scalar(B(self.tsc_i), B(self.ae_i), -1.0,
                                        f_e(254), AOP.mult, AOP.add))
            O(lambda e: e.tensor_tensor(B(self.t), B(self.am),
                                        B(self.tsc_i).bitcast(F32), AOP.mult))
            O(lambda e: e.tensor_scalar(B(self.b2), B(self.t), 2.0, E23,
                                        AOP.is_ge, AOP.mult))
            O(lambda e: e.tensor_scalar(B(self.madj), B(self.t), 1.0, E23,
                                        AOP.is_lt, AOP.mult))
            O(lambda e: e.tensor_tensor(B(self.madj), B(self.madj), B(self.b2),
                                        AOP.subtract))
        O(lambda e: e.tensor_tensor(E3(self.sa), x3,
                                    BB(self.ers_i).bitcast(F32), AOP.mult))
        O(lambda e: e.tensor_scalar(E(self.sb_i), E(self.sa).bitcast(I32),
                                    EXP_MASK, None, AOP.bitwise_and))
        if act:
            O(lambda e: e.scalar_tensor_tensor(E3(self.sc), E3(self.sb_i), 0.0,
                                               BB(self.madj), AOP.add, AOP.add))
            O(lambda e: e.tensor_scalar(E(self.sc), E(self.sc), f_e(127),
                                        f_e(129), AOP.max, AOP.min))
            O(lambda e: e.scalar_tensor_tensor(E3(self.sb_i), E3(self.sc), K_SN,
                                               BB(self.madj), AOP.add,
                                               AOP.subtract))
            O(lambda e: e.tensor_scalar(E(self.sd_i), E(self.sb_i), -1.0,
                                        -f_e(2), AOP.mult, AOP.add))
        else:
            O(lambda e: e.tensor_scalar(E(self.sc), E(self.sb_i), f_e(127),
                                        f_e(129), AOP.max, AOP.min))
            O(lambda e: e.tensor_scalar(E(self.sb_i), E(self.sc),
                                        -float(mfrac) * E23, None, AOP.add))
            O(lambda e: e.tensor_scalar(E(self.sd_i), E(self.sb_i), -1.0,
                                        f_e(254), AOP.mult, AOP.add))
        O(lambda e: e.tensor_tensor(E(self.se), E(self.sa),
                                    E(self.sd_i).bitcast(F32), AOP.mult))
        O(lambda e: e.tensor_scalar(E(self.sd_i), E(self.se), CMAGIC, -CMAGIC,
                                    AOP.add, AOP.add))
        if act:
            O(lambda e: e.tensor_tensor(E(self.sa), E(self.se), E(self.sd_i),
                                        AOP.subtract))
            O(lambda e: e.scalar_tensor_tensor(E(self.sc), E(self.sa), 0.5,
                                               E(self.sd_i), AOP.is_gt,
                                               AOP.add))
            O(lambda e: e.scalar_tensor_tensor(E(self.se), E(self.sa), -0.5,
                                               E(self.sc), AOP.is_lt,
                                               AOP.subtract))
            O(lambda e: e.scalar_tensor_tensor(E(self.sc), E(self.se), 0.0,
                                               E(self.sb_i).bitcast(F32),
                                               AOP.add, AOP.mult))
        else:
            O(lambda e: e.scalar_tensor_tensor(E(self.sc), E(self.sd_i), 0.0,
                                               E(self.sb_i).bitcast(F32),
                                               AOP.add, AOP.mult))
        O(lambda e: e.tensor_scalar(E(self.sc), E(self.sc), max_norm,
                                    -max_norm, AOP.min, AOP.max))
        return O(lambda e: e.tensor_tensor(
            out_ap.rearrange("p (n b) -> p n b", b=32), E3(self.sc),
            BB(self.es_i).bitcast(F32), AOP.mult))


def build_nc(phases=("pwa", "pwb", "p0", "p1", "p1b", "p2"), dbg_outs=(),
             dump=()):
    nc = bass.Bass("TRN2", target_bir_lowering=False)

    def din(name, shp, dt):
        return nc.dram_tensor(name, shp, dt, kind="ExternalInput")

    def dscr(name, shp, dt):
        kind = "ExternalOutput" if name in dbg_outs else "Internal"
        return nc.dram_tensor(name, shp, dt, kind=kind)

    x_d = din("x", [S, H], F32)
    wq_d = din("wq", [H, H], F32)
    wkv_d = din("wkv", [H, 2 * G * DH], F32)
    wd_d = din("wd", [H, H], F32)
    wg_d = din("wg", [II, H], F32)
    wu_d = din("wu", [II, H], F32)
    wdn_d = din("wdn", [H, II], F32)
    id_d = din("ident", [128, 128], BF16)
    mk_d = din("mask", [128, 128], F32)
    out_d = nc.dram_tensor("out", [S, H], F32, kind="ExternalOutput")

    wq_t = dscr("wq_t", [NH, NH, 128, 128], BF16)
    wkv_t = dscr("wkv_t", [8, NH, 128, 128], BF16)
    wd_b = dscr("wd_b", [H, H], BF16)
    qwg_t = dscr("qwg_t", [NO, NH, 128, 512], FP8)
    qwu_t = dscr("qwu_t", [NO, NH, 128, 512], FP8)
    qwd_t = dscr("qwd_t", [NH, NOC, 128, 128], FP8)

    P = Plan()
    top = contextlib.ExitStack()
    with top:
        sems = {k: top.enter_context(nc.semaphore(f"sem_{k}"))
                for k in ("v", "s", "t", "sync", "gp")}
        tp_ps = [top.enter_context(nc.psum_tensor(f"tp{i}", [128, 512], BF16))
                 for i in range(2)]
        sbL = lambda n, shp, dt: top.enter_context(
            nc.sbuf_tensor(n, shp, dt, side="left"))
        ident = sbL("ident_sb", [128, 128], BF16)
        mask = sbL("mask_sb", [128, 128], F32)
        qxnT = sbL("qxnT", [128, NH * S], FP8)
        rsum = sbL("rsum", [128, NT], F32)
        sm = {k: sbL(f"sm_{k}", [128, 1], F32)
              for k in ("sums", "sq", "mean", "msq", "var", "rstd", "negb",
                        "mx", "nb", "sume")}

        c0 = P.dma("sync", lambda e: e.dma_start(ident[:, :], id_d[:, :]))
        c0 = P.dma("sync", lambda e: e.dma_start(mask[:, :], mk_d[:, :]))

        mq_st = contextlib.ExitStack()
        mq = MXQuant(mq_st, nc, "mq", 128, 1024, side="right")

        if "pwa" in phases:
            st, B = {}, {}
            stk = contextlib.ExitStack()
            for nm, shp, dt in [("wbufA", [128, H], F32), ("wbufB", [128, H], F32),
                                ("cbufA", [128, H], BF16), ("cbufB", [128, H], BF16)]:
                B[nm] = stk.enter_context(
                    nc.sbuf_tensor(f"pwa_{nm}", shp, dt, side="right"))
            jobs = []
            for r in range(NH):
                dst = wq_t.ap()[:, r, :, :].transpose([1, 0, 2])
                jobs.append((wq_d[r * 128:(r + 1) * 128, :], H, dst, NH))
            for r in range(NH):
                dst = wkv_t.ap()[:, r, :, :].transpose([1, 0, 2])
                jobs.append((wkv_d[r * 128:(r + 1) * 128, :], 2 * G * DH, dst, 8))
            for r in range(NH):
                jobs.append((wd_d[r * 128:(r + 1) * 128, :], H,
                             wd_b[r * 128:(r + 1) * 128, :], 0))
            last_cast = {}
            last_store = {}
            for k, (src, cols, dst, nch) in enumerate(jobs):
                wb = B["wbufA"] if k % 2 == 0 else B["wbufB"]
                cb = B["cbufA"] if k % 2 == 0 else B["cbufB"]
                w = [("s", last_cast.get(k - 2, 0))]
                lc = P.dma("sync", lambda e, s=src, d=wb, c=cols:
                           e.dma_start(d[:, :c], s), waits=w)
                pos = P.op("s", lambda e, a=wb, b=cb, c=cols:
                           e.activation(b[:, :c], a[:, :c], AFT.Copy),
                           waits=[("sync", lc), ("gp", last_store.get(k - 2, 0))])
                last_cast[k] = pos
                if nch:
                    sdst = dst
                    ssrc = cb[:, :cols].rearrange("p (c f) -> p c f", f=128)
                else:
                    sdst = dst
                    ssrc = cb[:, :cols]
                last_store[k] = P.dma(
                    "gp", lambda e, a=ssrc, d=sdst: e.dma_start(d, a),
                    waits=[("s", pos)])
            stk.close()
            P.barrier()

        if "pwb" in phases:
            stk = contextlib.ExitStack()
            B = {}
            for nm, shp, dt in [("wbufA", [128, H], F32), ("wbufB", [128, H], F32),
                                ("qbuf", [128, 1024], BF16),
                                ("stripes", [128, NH * 512], FP8)]:
                B[nm] = stk.enter_context(
                    nc.sbuf_tensor(f"pwb_{nm}", shp, dt, side="right"))
            state = {"qdone": {}, "bank_ev": [0, 0], "store": 0}

            def pwb_tile(src_ap, stripe_sub):
                wb = B["wbufA"] if state.get("flip", 0) == 0 else B["wbufB"]
                state["flip"] = 1 - state.get("flip", 0)
                w = [("v", state["qdone"].get(id(wb), 0))]
                lc = P.dma("sync", lambda e, s=src_ap, d=wb:
                           e.dma_start(d[:, :], s), waits=w)
                qpos = 0
                for half in range(2):
                    qpos = mq.emit(
                        P, wb[:, half * 1024:(half + 1) * 1024], B["qbuf"][:, :],
                        mfrac=1, max_norm=6.0, act=False, f=1024,
                        waits=([("sync", lc)] if half == 0 else
                               [("t", state.get("tpos", 0))]))
                    state["qdone"][id(wb)] = P.n["v"]
                    for grp4 in range(2):
                        bk = (half * 2 + grp4) % 2
                        bank = tp_ps[bk]
                        wt = [("s", state["bank_ev"][bk]), ("v", qpos)]
                        qb = B["qbuf"]
                        for j4 in range(4):
                            j = grp4 * 4 + j4
                            P.op("t", lambda e, j=j, bank=bank, j4=j4, qb=qb:
                                 e.transpose(bank[:, j4 * 128:(j4 + 1) * 128],
                                             qb[:, j * 128:(j + 1) * 128],
                                             ident[:, :]),
                                 inc=(j4 == 3), waits=(wt if j4 == 0 else ()))
                        state["tpos"] = P.n["t"]
                        jg = half * 8 + grp4 * 4
                        sv = B["stripes"].ap().rearrange("p (j f) -> p j f",
                                                         f=512)
                        dst = sv[:, jg // 4 * 4:jg // 4 * 4 + 4,
                                 stripe_sub * 128:(stripe_sub + 1) * 128]
                        state["bank_ev"][bk] = P.op(
                            "s", lambda e, bank=bank, dst=dst:
                            e.activation(dst.rearrange("p j f -> p (j f)")
                                         if False else dst,
                                         bank[:, :].rearrange(
                                             "p (j f) -> p j f", f=128),
                                         AFT.Copy),
                            waits=[("t", state["tpos"])])

            def stripes_store(dst_fn):
                spos = P.n["s"]
                state["store"] = P.dma("gp", dst_fn, waits=[("s", spos)])
                P.wait("s", "gp", state["store"])

            for ot in range(NO):
                for sub in range(4):
                    rows = (ot * 4 + sub) * 128
                    pwb_tile(wg_d[rows:rows + 128, :], sub)
                stripes_store(lambda e, ot=ot, sp=B["stripes"]: e.dma_start(
                    qwg_t.ap()[ot].transpose([1, 0, 2]),
                    sp[:, :].rearrange("p (c f) -> p c f", f=512)))
            for ot in range(NO):
                for sub in range(4):
                    rows = (ot * 4 + sub) * 128
                    pwb_tile(wu_d[rows:rows + 128, :], sub)
                stripes_store(lambda e, ot=ot, sp=B["stripes"]: e.dma_start(
                    qwu_t.ap()[ot].transpose([1, 0, 2]),
                    sp[:, :].rearrange("p (c f) -> p c f", f=512)))
            for hs in range(4):
                for c in range(4):
                    for sub in range(4):
                        rows = (hs * 4 + sub) * 128
                        pwb_tile(wdn_d[rows:rows + 128, c * H:(c + 1) * H], sub)
                    for qq in range(4):
                        dst = qwd_t.ap()[hs * 4 + qq,
                                         c * NH:(c + 1) * NH, :, :]\
                            .transpose([1, 0, 2])
                        src = B["stripes"].ap().rearrange(
                            "p (j w) -> p j w", w=512)[:, :,
                                                       qq * 128:(qq + 1) * 128]
                        stripes_store(lambda e, dst=dst, src=src:
                                      e.dma_start(dst, src))
            stk.close()
            P.barrier()

        stL = contextlib.ExitStack()
        stKV = contextlib.ExitStack()
        if "p0" in phases:
            qT = stL.enter_context(nc.sbuf_tensor("qT", [128, NH * S], BF16,
                                                  side="left"))
            Ksb = stKV.enter_context(nc.sbuf_tensor("Ksb", [128, G * S], BF16,
                                                    side="left"))
            Vtm = stKV.enter_context(nc.sbuf_tensor("Vtm", [128, NT * 512],
                                                    BF16, side="left"))
            stk = contextlib.ExitStack()
            B = {}
            for nm, shp, dt in [("xbuf", [128, H], F32), ("xn32", [128, 1024], F32),
                                ("xnbf", [128, H], BF16),
                                ("xnT", [128, NH * 512], BF16),
                                ("wqbA", [128, NH * 128], BF16),
                                ("wqbB", [128, NH * 128], BF16)]:
                B[nm] = stk.enter_context(
                    nc.sbuf_tensor(f"p0_{nm}", shp, dt, side="right"))
            pj_ps = [stk.enter_context(nc.psum_tensor(f"pj{i}", [128, 512], F32))
                     for i in range(2)]
            st0 = {"bank_ev": [0, 0], "x_users": (0, 0), "xnT_ev": 0,
                   "qxnT_ev": 0, "pj_ev": [0, 0], "wq_used": [0, 0]}

            def transpose16(src_bf, dst_view_fn, out_dt_note, first_wait):
                last_ev = 0
                for grp4 in range(4):
                    bk = grp4 % 2
                    bank = tp_ps[bk]
                    wt = [("s", st0["bank_ev"][bk])]
                    if grp4 == 0:
                        wt += first_wait
                    for j4 in range(4):
                        j = grp4 * 4 + j4
                        P.op("t", lambda e, j=j, bank=bank, j4=j4, s=src_bf:
                             e.transpose(bank[:, j4 * 128:(j4 + 1) * 128],
                                         s[:, j * 128:(j + 1) * 128],
                                         ident[:, :]),
                             inc=(j4 == 3), waits=(wt if j4 == 0 else ()))
                    tpos = P.n["t"]
                    dst = dst_view_fn(grp4)
                    ev = P.op("s", lambda e, bank=bank, dst=dst:
                              e.activation(dst, bank[:, :].rearrange(
                                  "p (j f) -> p j f", f=128), AFT.Copy),
                              waits=[("t", tpos)])
                    st0["bank_ev"][bk] = ev
                    last_ev = ev
                return last_ev

            for g4 in range(4):
                for tl in range(4):
                    tt = g4 * 4 + tl
                    lx = P.dma("sync", lambda e, tt=tt:
                               e.dma_start(B["xbuf"][:, :],
                                           x_d[tt * 128:(tt + 1) * 128, :]),
                               waits=[("v", st0["x_users"][0]),
                                      ("s", st0["x_users"][1])])
                    P.op("v", lambda e: e.tensor_reduce(
                        sm["sums"][:, :], B["xbuf"][:, :], AXL.X, AOP.add),
                        waits=[("sync", lx)])
                    sqp = P.op("s", lambda e: e.activation(
                        mq.se[:, :1024], B["xbuf"][:, :1024], AFT.Square,
                        accum_out=sm["sq"][:, :1]), waits=[("sync", lx)])
                    sqp = P.op("s", lambda e: e.activation(
                        mq.sa[:, :1024], B["xbuf"][:, 1024:], AFT.Square,
                        accum_out=sm["mx"][:, :1]))
                    P.op("v", lambda e: e.tensor_scalar(
                        sm["mean"][:, :], sm["sums"][:, :], 1.0 / H, None,
                        AOP.mult))
                    P.op("v", lambda e: e.tensor_tensor(
                        sm["msq"][:, :], sm["mean"][:, :], sm["mean"][:, :],
                        AOP.mult))
                    P.op("v", lambda e: e.tensor_tensor(
                        sm["sq"][:, :], sm["sq"][:, :], sm["mx"][:, :],
                        AOP.add), waits=[("s", sqp)])
                    P.op("v", lambda e: e.scalar_tensor_tensor(
                        sm["var"][:, :], sm["sq"][:, :], 1.0 / H,
                        sm["msq"][:, :], AOP.mult, AOP.subtract))
                    vp = P.op("v", lambda e: e.tensor_scalar(
                        sm["var"][:, :], sm["var"][:, :], 1e-5, None, AOP.add))
                    rp = P.op("s", lambda e: e.activation(
                        sm["var"][:, :], sm["var"][:, :], AFT.Sqrt),
                        waits=[("v", vp)])
                    P.op("v", lambda e: e.reciprocal(
                        sm["rstd"][:, :], sm["var"][:, :]), waits=[("s", rp)])
                    nbp = P.op("v", lambda e: e.scalar_tensor_tensor(
                        sm["negb"][:, :], sm["mean"][:, :], -1.0,
                        sm["rstd"][:, :], AOP.mult, AOP.mult))
                    xnbfp = P.op("s", lambda e: e.activation(
                        B["xnbf"][:, :], B["xbuf"][:, :], AFT.Identity,
                        bias=sm["negb"][:, :1], scale=sm["rstd"][:, :1]),
                        waits=[("v", nbp)])
                    xt_ev = transpose16(
                        B["xnbf"],
                        lambda grp4, tl=tl: B["xnT"].ap().rearrange(
                            "p (j f) -> p j f", f=512)
                        [:, grp4 * 4:(grp4 + 1) * 4, tl * 128:(tl + 1) * 128],
                        BF16, [("s", xnbfp)])
                    st0["xnT_ev"] = xt_ev
                    qpos = 0
                    for half in range(2):
                        hp = P.op("s", lambda e, half=half: e.activation(
                            B["xn32"][:, :], B["xbuf"][:, half * 1024:
                                                       (half + 1) * 1024],
                            AFT.Identity, bias=sm["negb"][:, :1],
                            scale=sm["rstd"][:, :1]),
                            waits=[("v", qpos)])
                        qpos = mq.emit(
                            P, B["xn32"][:, :],
                            B["xnbf"][:, half * 1024:(half + 1) * 1024],
                            mfrac=3, max_norm=7.5, act=True, f=1024,
                            waits=[("s", hp), ("t", P.n["t"])])
                    st0["x_users"] = (qpos, P.n["s"])
                    st0["qxnT_ev"] = transpose16(
                        B["xnbf"],
                        lambda grp4, tt=tt: qxnT.ap().rearrange(
                            "p (j f) -> p j f", f=S)
                        [:, grp4 * 4:(grp4 + 1) * 4, tt * 128:(tt + 1) * 128],
                        FP8, [("v", qpos)])
                xnT_ready = st0["xnT_ev"]
                for et in range(NH + 8):
                    is_q = et < NH
                    eti = et if is_q else et - NH
                    wbuf = B["wqbA"] if et % 2 == 0 else B["wqbB"]
                    scr = wq_t if is_q else wkv_t
                    lw = P.dma("sync", lambda e, scr=scr, eti=eti, wbuf=wbuf:
                               e.dma_start(
                                   wbuf[:, :].rearrange("p (c f) -> p c f",
                                                        f=128),
                                   scr.ap()[eti].transpose([1, 0, 2])),
                               waits=[("t", st0["wq_used"][et % 2])])
                    bk = et % 2
                    wt = [("sync", lw), ("s", max(xnT_ready, st0["pj_ev"][bk]))]
                    for hc in range(NH):
                        P.op("t", lambda e, hc=hc, wbuf=wbuf, bk=bk:
                             e.matmul(pj_ps[bk][:, :],
                                      wbuf[:, hc * 128:(hc + 1) * 128],
                                      B["xnT"][:, hc * 512:(hc + 1) * 512],
                                      start=(hc == 0), stop=(hc == NH - 1)),
                             inc=(hc == NH - 1), waits=(wt if hc == 0 else ()))
                    st0["wq_used"][et % 2] = P.n["t"]
                    mmpos = P.n["t"]
                    if is_q:
                        dst = qT[:, eti * S + g4 * 512: eti * S + g4 * 512 + 512]
                        st0["pj_ev"][bk] = P.op(
                            "s", lambda e, dst=dst, bk=bk:
                            e.activation(dst, pj_ps[bk][:, :], AFT.Copy),
                            waits=[("t", mmpos)])
                    elif eti < 4:
                        dst = Ksb[:, eti * S + g4 * 512: eti * S + g4 * 512 + 512]
                        st0["pj_ev"][bk] = P.op(
                            "s", lambda e, dst=dst, bk=bk:
                            e.activation(dst, pj_ps[bk][:, :], AFT.Copy),
                            waits=[("t", mmpos)])
                    else:
                        g = eti - 4
                        vev = P.op("s", lambda e, bk=bk: e.activation(
                            B["xnbf"][:, :512], pj_ps[bk][:, :], AFT.Copy),
                            waits=[("t", mmpos)])
                        st0["pj_ev"][bk] = vev
                        bank = tp_ps[0]
                        wt2 = [("s", max(st0["bank_ev"][0], vev))]
                        for i in range(4):
                            P.op("t", lambda e, i=i, bank=bank:
                                 e.transpose(bank[:, i * 128:(i + 1) * 128],
                                             B["xnbf"][:, i * 128:(i + 1) * 128],
                                             ident[:, :]),
                                 inc=(i == 3), waits=(wt2 if i == 0 else ()))
                        tpos = P.n["t"]
                        dst = Vtm.ap().rearrange("p (tb f) -> p tb f", f=512)\
                            [:, g4 * 4:(g4 + 1) * 4, g * 128:(g + 1) * 128]
                        st0["bank_ev"][0] = P.op(
                            "s", lambda e, bank=bank, dst=dst:
                            e.activation(dst, bank[:, :].rearrange(
                                "p (j f) -> p j f", f=128), AFT.Copy),
                            waits=[("t", tpos)])
            stk.close()
            P.barrier()
            if "qxnT" in dump:
                dq = nc.dram_tensor("dbg_qxnT", [128, NH * S], FP8,
                                    kind="ExternalOutput")
                P.dma("gp", lambda e: e.dma_start(dq[:, :], qxnT[:, :]))
                P.barrier()

        if "p1" in phases:
            stk = contextlib.ExitStack()
            Pbuf = stk.enter_context(nc.sbuf_tensor("Pbuf", [128, S], BF16,
                                                    side="right"))
            PT = stk.enter_context(nc.sbuf_tensor("PT", [128, S], BF16,
                                                  side="right"))
            sc_ps = stk.enter_context(nc.psum_tensor("sc_ps", [128, S], F32))
            cx_ps = stk.enter_context(nc.psum_tensor("cx_ps", [128, 512], F32))
            s1 = {"cons_v": 0, "cons_s": 0, "pt_ev": 0, "cx_ev": 0,
                  "pt_used": 0}
            for h in range(HEADS):
                g = h // (HEADS // G)
                for tq in range(NT):
                    ncov = tq + 1
                    W = ncov * 128
                    nkb = (W + 511) // 512
                    wt = [("v", s1["cons_v"]), ("s", s1["cons_s"])]
                    for kb in range(nkb):
                        n = min(512, W - kb * 512)
                        P.op("t", lambda e, kb=kb, n=n, h=h, tq=tq, g=g:
                             e.matmul(sc_ps[:, kb * 512: kb * 512 + n],
                                      qT[:, h * S + tq * 128:
                                         h * S + tq * 128 + 128],
                                      Ksb[:, g * S + kb * 512:
                                          g * S + kb * 512 + n],
                                      start=True, stop=True),
                             inc=(kb == nkb - 1), waits=(wt if kb == 0 else ()))
                    qkpos = P.n["t"]
                    mp = P.op("v", lambda e, tq=tq: e.tensor_tensor(
                        sc_ps[:, tq * 128:(tq + 1) * 128],
                        sc_ps[:, tq * 128:(tq + 1) * 128], mask[:, :],
                        AOP.add), waits=[("t", qkpos)])
                    P.op("v", lambda e, W=W: e.tensor_reduce(
                        sm["mx"][:, :], sc_ps[:, :W], AXL.X, AOP.max))
                    nbp = P.op("v", lambda e: e.tensor_scalar(
                        sm["nb"][:, :], sm["mx"][:, :], -SCALE, None,
                        AOP.mult))
                    s1["cons_v"] = nbp
                    ep = P.op("s", lambda e, W=W: e.activation(
                        Pbuf[:, :W], sc_ps[:, :W], AFT.Exp,
                        bias=sm["nb"][:, :1], scale=SCALE,
                        accum_out=sm["sume"][:, :1]), waits=[("v", nbp)])
                    s1["cons_s"] = ep
                    rp1 = P.op("v", lambda e: e.reciprocal(
                        rsum[:, 0:1], sm["sume"][:, :]),
                        waits=[("s", ep)])
                    ep = P.op("s", lambda e, W=W: e.activation(
                        Pbuf[:, :W], Pbuf[:, :W], AFT.Identity,
                        scale=rsum[:, 0:1]), waits=[("v", rp1)])
                    s1["cons_s"] = ep
                    ptev = 0
                    for g4 in range((ncov + 3) // 4):
                        nj = min(4, ncov - g4 * 4)
                        bk = g4 % 2
                        bank = tp_ps[bk]
                        wt2 = [("s", max(ep, s1.get(f"bkev{bk}", 0)))]
                        if g4 == 0:
                            wt2.append(("t", s1["pt_used"]))
                        for j in range(nj):
                            kb = g4 * 4 + j
                            P.op("t", lambda e, kb=kb, j=j, bank=bank:
                                 e.transpose(bank[:, j * 128:(j + 1) * 128],
                                             Pbuf[:, kb * 128:(kb + 1) * 128],
                                             ident[:, :]),
                                 inc=(j == nj - 1),
                                 waits=(wt2 if j == 0 else ()))
                        tpos = P.n["t"]
                        ptev = P.op("s", lambda e, bank=bank, g4=g4, nj=nj:
                                    e.activation(
                                        PT[:, g4 * 512: g4 * 512 + nj * 128],
                                        bank[:, :nj * 128], AFT.Copy),
                                    waits=[("t", tpos)])
                        s1[f"bkev{bk}"] = ptev
                    wt3 = [("s", ptev), ("s", s1["cx_ev"])]
                    for kb in range(ncov):
                        P.op("t", lambda e, kb=kb, g=g:
                             e.matmul(cx_ps[:, :128],
                                      Vtm[:, kb * 512 + g * 128:
                                          kb * 512 + g * 128 + 128],
                                      PT[:, kb * 128:(kb + 1) * 128],
                                      start=(kb == 0), stop=(kb == ncov - 1)),
                             inc=(kb == ncov - 1),
                             waits=(wt3 if kb == 0 else ()))
                    pvpos = P.n["t"]
                    s1["pt_used"] = pvpos
                    s1["cx_ev"] = P.op(
                        "s", lambda e, h=h, tq=tq: e.activation(
                            qT[:, h * S + tq * 128: h * S + tq * 128 + 128],
                            cx_ps[:, :128], AFT.Copy), waits=[("t", pvpos)])
            stk.close()
            P.barrier()

        stKV.close()
        mq_st.close()

        if "p1b" in phases:
            stk = contextlib.ExitStack()
            wdsb = stk.enter_context(nc.sbuf_tensor("wdsb", [128, NH * H],
                                                    BF16, side="right"))
            xrA = stk.enter_context(nc.sbuf_tensor("xrA", [128, H], F32,
                                                   side="right"))
            xrB = stk.enter_context(nc.sbuf_tensor("xrB", [128, H], F32,
                                                   side="right"))
            obuf = stk.enter_context(nc.sbuf_tensor("obuf", [128, H], F32,
                                                    side="right"))
            at_ps = stk.enter_context(nc.psum_tensor("at_ps", [128, H], F32))
            lw = 0
            for r in range(NH):
                lw = P.dma("sync", lambda e, r=r: e.dma_start(
                    wdsb[:, r * H:(r + 1) * H],
                    wd_b[r * 128:(r + 1) * 128, :]))
            sb1 = {"stt": 0, "store": 0, "lx": {}, "stt_hist": {}}

            def load_x(tq):
                if tq >= NT:
                    return
                xr = xrA if tq % 2 == 0 else xrB
                sb1["lx"][tq] = P.dma(
                    "sync", lambda e, tq=tq, xr=xr: e.dma_start(
                        xr[:, :], x_d[tq * 128:(tq + 1) * 128, :]),
                    waits=[("v", sb1["stt_hist"].get(tq - 2, 0))])
            load_x(0)
            load_x(1)
            for tq in range(NT):
                xr = xrA if tq % 2 == 0 else xrB
                wt = [("sync", lw), ("v", sb1["stt"])]
                for ec in range(NH):
                    for hq in range(4):
                        P.op("t", lambda e, ec=ec, hq=hq, tq=tq:
                             e.matmul(at_ps[:, hq * 512:(hq + 1) * 512],
                                      qT[:, ec * S + tq * 128:
                                         ec * S + tq * 128 + 128],
                                      wdsb[:, ec * H + hq * 512:
                                           ec * H + hq * 512 + 512],
                                      start=(ec == 0), stop=(ec == NH - 1)),
                             inc=(ec == NH - 1 and hq == 3),
                             waits=(wt if ec == 0 and hq == 0 else ()))
                mmpos = P.n["t"]
                sb1["stt"] = P.op("v", lambda e, tq=tq, xr=xr:
                                  e.tensor_tensor(
                                      obuf[:, :], at_ps[:, :], xr[:, :],
                                      AOP.add),
                                  waits=[("t", mmpos),
                                         ("sync", sb1["lx"][tq]),
                                         ("gp", sb1["store"])])
                sb1["stt_hist"][tq] = sb1["stt"]
                sb1["store"] = P.dma("gp", lambda e, tq=tq: e.dma_start(
                    out_d[tq * 128:(tq + 1) * 128, :], obuf[:, :]),
                    waits=[("v", sb1["stt"])])
                load_x(tq + 2)
            stk.close()
            P.barrier()
        stL.close()

        if "p2" in phases:
            stk = contextlib.ExitStack()
            sbR = lambda n, shp, dt: stk.enter_context(
                nc.sbuf_tensor(n, shp, dt, side="right"))
            oacc = stL2 = stk.enter_context(nc.sbuf_tensor(
                "oacc", [128, 4 * H], F32, side="left"))
            qintT = stk.enter_context(nc.sbuf_tensor(
                "qintT", [128, NOC * 512], FP8, side="left"))
            wgA = sbR("wgA", [128, NH * 512], FP8)
            wgB = sbR("wgB", [128, NH * 512], FP8)
            wuA = sbR("wuA", [128, NH * 512], FP8)
            wuB = sbR("wuB", [128, NH * 512], FP8)
            wdA = sbR("wdA", [128, NOC * 128], FP8)
            wdB = sbR("wdB", [128, NOC * 128], FP8)
            sg = sbR("sg", [128, 512], F32)
            ibuf = sbR("ibuf", [128, 512], F32)
            qib = sbR("qib", [128, 512], BF16)
            fT = sbR("fT", [128, 512], BF16)
            mq2_st = contextlib.ExitStack()
            mq2 = MXQuant(mq2_st, nc, "mq2", 128, 512, side="right")
            g_ps = stk.enter_context(nc.psum_tensor("g_ps", [128, 512], F32))
            u_ps = stk.enter_context(nc.psum_tensor("u_ps", [128, 512], F32))
            d_ps = stk.enter_context(nc.psum_tensor("d_ps", [128, 512], F32))
            tq_ps = stk.enter_context(nc.psum_tensor("tq_ps", [128, 512], BF16))
            tr_ps = stk.enter_context(nc.psum_tensor("tr_ps", [128, 512], BF16))
            s2 = {"oacc_v": 0, "store": 0, "lg_used": [0, 0], "lu_used": [0, 0],
                  "ld_used": [0, 0], "silu": 0, "mul": 0, "qev": 0, "qtp": 0,
                  "dev": 0, "ftp": 0, "addv": 0}
            for qtr in range(4):
                lo = 0
                for i in range(4):
                    lo = P.dma("sync", lambda e, qtr=qtr, i=i: e.dma_start(
                        oacc[:, i * H:(i + 1) * H],
                        out_d[(qtr * 4 + i) * 128:(qtr * 4 + i + 1) * 128, :]),
                        waits=[("v", s2["addv"]), ("gp", s2["store"])])
                for ot in range(NO):
                    bk = ot % 2
                    wg = wgA if bk == 0 else wgB
                    wu = wuA if bk == 0 else wuB
                    lg = P.dma("sync", lambda e, ot=ot, wg=wg: e.dma_start(
                        wg[:, :].rearrange("p (c f) -> p c f", f=512),
                        qwg_t.ap()[ot].transpose([1, 0, 2])),
                        waits=[("t", s2["lg_used"][bk])])
                    lu = P.dma("sync", lambda e, ot=ot, wu=wu: e.dma_start(
                        wu[:, :].rearrange("p (c f) -> p c f", f=512),
                        qwu_t.ap()[ot].transpose([1, 0, 2])),
                        waits=[("t", s2["lu_used"][bk])])
                    for tl in range(4):
                        gt = qtr * 4 + tl
                        wt = [("sync", lg), ("s", s2["silu"])]
                        for hc in range(NH):
                            P.op("t", lambda e, hc=hc, gt=gt, wg=wg:
                                 e.matmul(g_ps[:, :],
                                          qxnT[:, hc * S + gt * 128:
                                               hc * S + gt * 128 + 128],
                                          wg[:, hc * 512:(hc + 1) * 512],
                                          start=(hc == 0), stop=(hc == NH - 1)),
                                 inc=(hc == NH - 1),
                                 waits=(wt if hc == 0 else ()))
                        gpos = P.n["t"]
                        s2["lg_used"][bk] = gpos
                        wt = [("sync", lu), ("v", s2["mul"])]
                        for hc in range(NH):
                            P.op("t", lambda e, hc=hc, gt=gt, wu=wu:
                                 e.matmul(u_ps[:, :],
                                          qxnT[:, hc * S + gt * 128:
                                               hc * S + gt * 128 + 128],
                                          wu[:, hc * 512:(hc + 1) * 512],
                                          start=(hc == 0), stop=(hc == NH - 1)),
                                 inc=(hc == NH - 1),
                                 waits=(wt if hc == 0 else ()))
                        upos = P.n["t"]
                        s2["lu_used"][bk] = upos
                        s2["silu"] = P.op("s", lambda e: e.activation(
                            sg[:, :], g_ps[:, :], AFT.Silu),
                            waits=[("t", gpos)])
                        s2["mul"] = P.op("v", lambda e: e.tensor_tensor(
                            ibuf[:, :], sg[:, :], u_ps[:, :], AOP.mult),
                            waits=[("s", s2["silu"]), ("t", upos)])
                        qpos = mq2.emit(P, ibuf[:, :], qib[:, :], mfrac=3,
                                        max_norm=7.5, act=True, f=512,
                                        waits=[("t", s2["qtp"])])
                        wt2 = [("v", qpos), ("s", s2["qev"])]
                        for j in range(4):
                            P.op("t", lambda e, j=j:
                                 e.transpose(tq_ps[:, j * 128:(j + 1) * 128],
                                             qib[:, j * 128:(j + 1) * 128],
                                             ident[:, :]),
                                 inc=(j == 3), waits=(wt2 if j == 0 else ()))
                        s2["qtp"] = P.n["t"]
                        dst = qintT.ap().rearrange("p (c f) -> p c f", f=512)\
                            [:, ot * 4:(ot + 1) * 4, tl * 128:(tl + 1) * 128]
                        s2["qev"] = P.op(
                            "s", lambda e, dst=dst: e.activation(
                                dst, tq_ps[:, :].rearrange(
                                    "p (j f) -> p j f", f=128), AFT.Copy),
                            waits=[("t", s2["qtp"])])
                qint_ready = s2["qev"]
                if "qintT" in dump and qtr == 0:
                    dqi = nc.dram_tensor("dbg_qintT", [128, NOC * 512], FP8,
                                         kind="ExternalOutput")
                    P.dma("gp", lambda e: e.dma_start(dqi[:, :], qintT[:, :]),
                          waits=[("s", qint_ready)])
                for ht in range(NH):
                    bk = ht % 2
                    wd2 = wdA if bk == 0 else wdB
                    ld = P.dma("sync", lambda e, ht=ht, wd2=wd2: e.dma_start(
                        wd2[:, :].rearrange("p (c f) -> p c f", f=128),
                        qwd_t.ap()[ht].transpose([1, 0, 2])),
                        waits=[("t", s2["ld_used"][bk])])
                    wt = [("sync", ld), ("s", max(qint_ready, s2["dev"]))]
                    for oc in range(NOC):
                        P.op("t", lambda e, oc=oc, wd2=wd2:
                             e.matmul(d_ps[:, :],
                                      wd2[:, oc * 128:(oc + 1) * 128],
                                      qintT[:, oc * 512:(oc + 1) * 512],
                                      start=(oc == 0), stop=(oc == NOC - 1)),
                             inc=(oc == NOC - 1), waits=(wt if oc == 0 else ()))
                    dpos = P.n["t"]
                    s2["ld_used"][bk] = dpos
                    s2["dev"] = P.op("s", lambda e: e.activation(
                        fT[:, :], d_ps[:, :], AFT.Copy), waits=[("t", dpos)])
                    wt2 = [("s", s2["dev"]), ("v", s2["addv"])]
                    for j in range(4):
                        P.op("t", lambda e, j=j:
                             e.transpose(tr_ps[:, j * 128:(j + 1) * 128],
                                         fT[:, j * 128:(j + 1) * 128],
                                         ident[:, :]),
                             inc=(j == 3), waits=(wt2 if j == 0 else ()))
                    s2["ftp"] = P.n["t"]
                    dst = oacc.ap().rearrange("p (i f) -> p i f", f=H)\
                        [:, :, ht * 128:(ht + 1) * 128]
                    s2["addv"] = P.op(
                        "v", lambda e, dst=dst: e.tensor_tensor(
                            dst, tr_ps[:, :].rearrange("p (j f) -> p j f",
                                                       f=128), dst, AOP.add),
                        waits=[("t", s2["ftp"]), ("sync", lo)])
                for i in range(4):
                    s2["store"] = P.dma("gp", lambda e, qtr=qtr, i=i:
                                        e.dma_start(
                                            out_d[(qtr * 4 + i) * 128:
                                                  (qtr * 4 + i + 1) * 128, :],
                                            oacc[:, i * H:(i + 1) * H]),
                                        waits=[("v", s2["addv"])])
            mq2_st.close()
            stk.close()

        P.wait("gp", "gp", P.nd["gp"])
        P.barrier()

        with nc.Block() as block:
            @block.sync
            def _(e):
                for st_ in P.steps["sync"]:
                    st_(e, sems)

            @block.vector
            def _(e):
                for st_ in P.steps["v"]:
                    st_(e, sems)

            @block.scalar
            def _(e):
                for st_ in P.steps["s"]:
                    st_(e, sems)

            @block.tensor
            def _(e):
                for st_ in P.steps["t"]:
                    st_(e, sems)

            @block.gpsimd
            def _(e):
                for st_ in P.steps["gp"]:
                    st_(e, sems)

    return nc


def make_consts():
    ident = np.eye(128, dtype=ml_dtypes.bfloat16)
    mask = np.triu(np.full((128, 128), -1e30, dtype=np.float32), k=1)
    return ident, mask


def kernel(L0_attn_x, L0_attn_w_q, L0_attn_w_kv, L0_attn_w_dense,
           L0_ffn_w_gate, L0_ffn_w_up, L0_ffn_w_down):
    ident, mask = make_consts()
    xs = np.ascontiguousarray(np.asarray(L0_attn_x, dtype=np.float32))
    base = {
        "wq": np.ascontiguousarray(L0_attn_w_q, dtype=np.float32),
        "wkv": np.ascontiguousarray(L0_attn_w_kv, dtype=np.float32),
        "wd": np.ascontiguousarray(L0_attn_w_dense, dtype=np.float32),
        "wg": np.ascontiguousarray(L0_ffn_w_gate, dtype=np.float32),
        "wu": np.ascontiguousarray(L0_ffn_w_up, dtype=np.float32),
        "wdn": np.ascontiguousarray(L0_ffn_w_down, dtype=np.float32),
        "ident": ident, "mask": mask,
    }
    nc = build_nc()
    in_maps = [dict(base, x=xs[c]) for c in range(8)]
    res = run_bass_kernel_spmd(nc, in_maps, core_ids=list(range(8)))
    return np.stack([r["out"] for r in res.results], axis=0)


# version 02
# speedup vs baseline: 1.2615x; 1.2615x over previous
import contextlib
import math
import numpy as np
import ml_dtypes
import concourse.bass as bass
import concourse.mybir as mybir
from concourse.bass_utils import run_bass_kernel_spmd

F32 = mybir.dt.float32
I32 = mybir.dt.int32
BF16 = mybir.dt.bfloat16
FP8 = mybir.dt.float8e4
AOP = mybir.AluOpType
AFT = mybir.ActivationFunctionType
AXL = mybir.AxisListType

CMAGIC = 3.0 * 2.0**22
EXP_MASK = 0x7F800000
E23 = float(1 << 23)

S = 2048
H = 2048
II = 8192
HEADS, G, DH = 16, 4, 128
SCALE = DH ** -0.5
NT = S // 128
NH = H // 128
NO = II // 512
NOC = II // 128


def f_e(e):
    return float(e) * E23


class Plan:

    ENGS = ("v", "s", "t", "g", "sync")

    def __init__(self):
        self.steps = {e: [] for e in self.ENGS}
        self.n = {e: 0 for e in ("v", "s", "t", "g")}
        self.nd = {"sync": 0, "gd": 0}

    def op(self, eng, fn, inc=None, waits=()):
        auto = eng in ("v", "s", "g")
        do_inc = auto if inc is None else inc
        nb = self.n[eng]
        wl = [(k, v) for (k, v) in waits if v > 0]

        def step(e, sems):
            for k, val in wl:
                e.wait_ge(sems[k], val)
            instr = fn(e)
            if auto and nb > 0:
                instr._wait_ge(sems[eng], nb)
            if do_inc:
                instr.then_inc(sems[eng], 1)
        self.steps[eng].append(step)
        if do_inc:
            self.n[eng] += 1
        return self.n[eng]

    def wait(self, eng, sem_key, value):
        if value <= 0:
            return
        def step(e, sems):
            e.wait_ge(sems[sem_key], value)
        self.steps[eng].append(step)

    def dma(self, q, fn, waits=()):
        wl = [(k, v) for (k, v) in waits if v > 0]
        sem_key = "sync" if q == "sync" else "gd"
        lane = "sync" if q == "sync" else "g"

        def step(e, sems):
            for k, val in wl:
                e.wait_ge(sems[k], val)
            fn(e).then_inc(sems[sem_key], 16)
        self.steps[lane].append(step)
        self.nd[sem_key] += 16
        return self.nd[sem_key]

    def barrier(self):
        marks = dict(self.n)
        dmarks = dict(self.nd)
        for eng in self.ENGS:
            for k, v in marks.items():
                if k != eng:
                    self.wait(eng, k, v)
            for k, v in dmarks.items():
                if not (k == "sync" and eng == "sync") and \
                   not (k == "gd" and eng == "g"):
                    self.wait(eng, k, v)


class MXQuant:

    def __init__(self, ctx, nc, name, p, fmax, side=None):
        self.p, self.fmax = p, fmax
        nb = fmax // 32
        kw = {"side": side} if side else {}
        sb = lambda n, shp, dt: ctx.enter_context(
            nc.sbuf_tensor(f"{name}_{n}", shp, dt, **kw))
        self.am = sb("am", [p, nb], F32)
        self.t = sb("t", [p, nb], F32)
        self.b2 = sb("b2", [p, nb], F32)
        self.madj = sb("madj", [p, nb], F32)
        self.ae_i = sb("ae", [p, nb], I32)
        self.es_i = sb("es", [p, nb], I32)
        self.tsc_i = sb("tsc", [p, nb], I32)
        self.ers_i = sb("ers", [p, nb], I32)
        self.sa = sb("sa", [p, fmax], F32)
        self.sc = sb("sc", [p, fmax], F32)
        self.se = sb("se", [p, fmax], F32)
        self.sb_i = sb("sbi", [p, fmax], I32)
        self.sd_i = sb("sdi", [p, fmax], I32)

    def emit(self, P, x_ap, out_ap, mfrac, max_norm, act, p=None, f=None,
             eng="v", waits=()):
        p = p if p is not None else self.p
        f = f if f is not None else self.fmax
        nb = f // 32
        K_SN = -float(mfrac) * E23 - 2147483648.0
        x3 = x_ap.rearrange("p (n b) -> p n b", b=32)

        def B(t):
            return t[:p, :nb]

        def BB(t):
            return t[:p, :nb].unsqueeze(2).broadcast_to([p, nb, 32])

        def E(t):
            return t[:p, :f]

        def E3(t):
            return t[:p, :f].rearrange("p (n b) -> p n b", b=32)

        O = lambda fn, w=(): P.op(eng, fn, waits=w)
        if eng == "v":
            O(lambda e: e.tensor_reduce(B(self.am), x3, AXL.X, AOP.max,
                                        apply_absolute_value=True), w=waits)
        else:
            O(lambda e: e.tensor_scalar(E(self.se), x_ap, -1.0, None,
                                        AOP.mult), w=waits)
            O(lambda e: e.tensor_tensor(E(self.se), E(self.se), x_ap, AOP.max))
            a3 = E3(self.se)
            for w_ in (16, 8, 4, 2):
                O(lambda e, w_=w_: e.tensor_tensor(
                    a3[:, :, 0:w_], a3[:, :, 0:w_], a3[:, :, w_:2 * w_],
                    AOP.max))
            O(lambda e: e.tensor_tensor(B(self.am), a3[:, :, 0:1].squeeze(2),
                                        a3[:, :, 1:2].squeeze(2), AOP.max))
        O(lambda e: e.tensor_scalar(B(self.am), B(self.am), 2.0**-98, 2.0**98,
                                    AOP.max, AOP.min))
        O(lambda e: e.tensor_scalar(B(self.ae_i), B(self.am).bitcast(I32),
                                    EXP_MASK, None, AOP.bitwise_and))
        O(lambda e: e.tensor_scalar(B(self.es_i), B(self.ae_i), -f_e(2), None,
                                    AOP.add))
        O(lambda e: e.tensor_scalar(B(self.ers_i), B(self.ae_i), -1.0,
                                    2147483648.0, AOP.mult, AOP.add))
        if act:
            O(lambda e: e.tensor_scalar(B(self.tsc_i), B(self.ae_i), -1.0,
                                        f_e(254), AOP.mult, AOP.add))
            O(lambda e: e.tensor_tensor(B(self.t), B(self.am),
                                        B(self.tsc_i).bitcast(F32), AOP.mult))
            O(lambda e: e.tensor_scalar(B(self.b2), B(self.t), 2.0, E23,
                                        AOP.is_ge, AOP.mult))
            O(lambda e: e.tensor_scalar(B(self.madj), B(self.t), 1.0, E23,
                                        AOP.is_lt, AOP.mult))
            O(lambda e: e.tensor_tensor(B(self.madj), B(self.madj), B(self.b2),
                                        AOP.subtract))
        O(lambda e: e.tensor_tensor(E3(self.sa), x3,
                                    BB(self.ers_i).bitcast(F32), AOP.mult))
        O(lambda e: e.tensor_scalar(E(self.sb_i), E(self.sa).bitcast(I32),
                                    EXP_MASK, None, AOP.bitwise_and))
        if act:
            O(lambda e: e.scalar_tensor_tensor(E3(self.sc), E3(self.sb_i), 0.0,
                                               BB(self.madj), AOP.add, AOP.add))
            O(lambda e: e.tensor_scalar(E(self.sc), E(self.sc), f_e(127),
                                        f_e(129), AOP.max, AOP.min))
            O(lambda e: e.scalar_tensor_tensor(E3(self.sb_i), E3(self.sc), K_SN,
                                               BB(self.madj), AOP.add,
                                               AOP.subtract))
            O(lambda e: e.tensor_scalar(E(self.sd_i), E(self.sb_i), -1.0,
                                        -f_e(2), AOP.mult, AOP.add))
        else:
            O(lambda e: e.tensor_scalar(E(self.sc), E(self.sb_i), f_e(127),
                                        f_e(129), AOP.max, AOP.min))
            O(lambda e: e.tensor_scalar(E(self.sb_i), E(self.sc),
                                        -float(mfrac) * E23, None, AOP.add))
            O(lambda e: e.tensor_scalar(E(self.sd_i), E(self.sb_i), -1.0,
                                        f_e(254), AOP.mult, AOP.add))
        O(lambda e: e.tensor_tensor(E(self.se), E(self.sa),
                                    E(self.sd_i).bitcast(F32), AOP.mult))
        O(lambda e: e.tensor_scalar(E(self.sd_i), E(self.se), CMAGIC, -CMAGIC,
                                    AOP.add, AOP.add))
        if act:
            O(lambda e: e.tensor_tensor(E(self.sa), E(self.se), E(self.sd_i),
                                        AOP.subtract))
            O(lambda e: e.scalar_tensor_tensor(E(self.sc), E(self.sa), 0.5,
                                               E(self.sd_i), AOP.is_gt,
                                               AOP.add))
            O(lambda e: e.scalar_tensor_tensor(E(self.se), E(self.sa), -0.5,
                                               E(self.sc), AOP.is_lt,
                                               AOP.subtract))
            O(lambda e: e.scalar_tensor_tensor(E(self.sc), E(self.se), 0.0,
                                               E(self.sb_i).bitcast(F32),
                                               AOP.add, AOP.mult))
        else:
            O(lambda e: e.scalar_tensor_tensor(E(self.sc), E(self.sd_i), 0.0,
                                               E(self.sb_i).bitcast(F32),
                                               AOP.add, AOP.mult))
        O(lambda e: e.tensor_scalar(E(self.sc), E(self.sc), max_norm,
                                    -max_norm, AOP.min, AOP.max))
        return O(lambda e: e.tensor_tensor(
            out_ap.rearrange("p (n b) -> p n b", b=32), E3(self.sc),
            BB(self.es_i).bitcast(F32), AOP.mult))


def build_nc(phases=("pwa", "pwb", "p0", "p1", "p1b", "p2"), dbg_outs=(),
             dump=()):
    nc = bass.Bass("TRN2", target_bir_lowering=False)

    def din(name, shp, dt):
        return nc.dram_tensor(name, shp, dt, kind="ExternalInput")

    def dscr(name, shp, dt):
        kind = "ExternalOutput" if name in dbg_outs else "Internal"
        return nc.dram_tensor(name, shp, dt, kind=kind)

    x_d = din("x", [S, H], F32)
    wq_d = din("wq", [H, H], F32)
    wkv_d = din("wkv", [H, 2 * G * DH], F32)
    wd_d = din("wd", [H, H], F32)
    wg_d = din("wg", [II, H], F32)
    wu_d = din("wu", [II, H], F32)
    wdn_d = din("wdn", [H, II], F32)
    id_d = din("ident", [128, 128], BF16)
    mk_d = din("mask", [128, 128], F32)
    out_d = nc.dram_tensor("out", [S, H], F32, kind="ExternalOutput")

    wq_t = dscr("wq_t", [NH, NH, 128, 128], BF16)
    wkv_t = dscr("wkv_t", [8, NH, 128, 128], BF16)
    wd_b = dscr("wd_b", [H, H], BF16)
    qwg_t = dscr("qwg_t", [NO, NH, 128, 512], FP8)
    qwu_t = dscr("qwu_t", [NO, NH, 128, 512], FP8)
    qwd_t = dscr("qwd_t", [NH, NOC, 128, 128], FP8)

    P = Plan()
    top = contextlib.ExitStack()
    with top:
        sems = {k: top.enter_context(nc.semaphore(f"sem_{k}"))
                for k in ("v", "s", "t", "g", "sync", "gd")}
        tp_ps = [top.enter_context(nc.psum_tensor(f"tp{i}", [128, 512], BF16))
                 for i in range(2)]
        sbL = lambda n, shp, dt: top.enter_context(
            nc.sbuf_tensor(n, shp, dt, side="left"))
        ident = sbL("ident_sb", [128, 128], BF16)
        mask = sbL("mask_sb", [128, 128], F32)
        qxnT = sbL("qxnT", [128, NH * S], FP8)
        rsum = sbL("rsum", [128, NT], F32)
        sm = {k: sbL(f"sm_{k}", [128, 1], F32)
              for k in ("sums", "sq", "mean", "msq", "var", "rstd", "negb",
                        "mx", "nb", "sume")}

        c0 = P.dma("sync", lambda e: e.dma_start(ident[:, :], id_d[:, :]))
        c0 = P.dma("sync", lambda e: e.dma_start(mask[:, :], mk_d[:, :]))

        mq_st = contextlib.ExitStack()
        mq = MXQuant(mq_st, nc, "mq", 128, 2048, side="right")
        pw_tp = top.enter_context(nc.psum_tensor("pw_tp", [128, 512], BF16))

        if "pwa" in phases:
            st, B = {}, {}
            stk = contextlib.ExitStack()
            for nm, shp, dt in [("wbufA", [128, H], F32), ("wbufB", [128, H], F32),
                                ("cbufA", [128, H], BF16), ("cbufB", [128, H], BF16)]:
                B[nm] = stk.enter_context(
                    nc.sbuf_tensor(f"pwa_{nm}", shp, dt, side="right"))
            jobs = []
            for r in range(NH):
                dst = wq_t.ap()[:, r, :, :].transpose([1, 0, 2])
                jobs.append((wq_d[r * 128:(r + 1) * 128, :], H, dst, NH))
            for r in range(NH):
                dst = wkv_t.ap()[:, r, :, :].transpose([1, 0, 2])
                jobs.append((wkv_d[r * 128:(r + 1) * 128, :], 2 * G * DH, dst, 8))
            for r in range(NH):
                jobs.append((wd_d[r * 128:(r + 1) * 128, :], H,
                             wd_b[r * 128:(r + 1) * 128, :], 0))
            last_cast = {}
            last_store = {}
            for k, (src, cols, dst, nch) in enumerate(jobs):
                wb = B["wbufA"] if k % 2 == 0 else B["wbufB"]
                cb = B["cbufA"] if k % 2 == 0 else B["cbufB"]
                w = [("s", last_cast.get(k - 2, 0))]
                lc = P.dma("sync", lambda e, s=src, d=wb, c=cols:
                           e.dma_start(d[:, :c], s), waits=w)
                pos = P.op("s", lambda e, a=wb, b=cb, c=cols:
                           e.activation(b[:, :c], a[:, :c], AFT.Copy),
                           waits=[("sync", lc), ("gd", last_store.get(k - 2, 0))])
                last_cast[k] = pos
                if nch:
                    sdst = dst
                    ssrc = cb[:, :cols].rearrange("p (c f) -> p c f", f=128)
                else:
                    sdst = dst
                    ssrc = cb[:, :cols]
                last_store[k] = P.dma(
                    "gp", lambda e, a=ssrc, d=sdst: e.dma_start(d, a),
                    waits=[("s", pos)])
            stk.close()
            P.barrier()

        pw_units = []
        pw_st = contextlib.ExitStack()
        if "pwb" in phases:
            PB = {}
            state = {"qdone": {}, "bank_ev": 0, "store": 0, "tpos": 0,
                     "flip": 0}

            def pwb_open():
                for nm, shp, dt in [("wbufA", [128, H], F32),
                                    ("wbufB", [128, H], F32),
                                    ("qbuf", [128, H], BF16),
                                    ("stripes", [128, NH * 512], FP8)]:
                    PB[nm] = pw_st.enter_context(
                        nc.sbuf_tensor(f"pwb_{nm}", shp, dt, side="right"))

            def pwb_tile(src_ap, stripe_sub):
                wb = PB["wbufA"] if state["flip"] == 0 else PB["wbufB"]
                state["flip"] = 1 - state["flip"]
                w = [("v", state["qdone"].get(id(wb), 0))]
                lc = P.dma("sync", lambda e, s=src_ap, d=wb:
                           e.dma_start(d[:, :], s), waits=w)
                qpos = mq.emit(P, wb[:, :], PB["qbuf"][:, :],
                               mfrac=1, max_norm=6.0, act=False, f=H,
                               waits=[("sync", lc),
                                      ("t", state["tpos"])])
                state["qdone"][id(wb)] = P.n["v"]
                for grp4 in range(4):
                    wt = [("s", state["bank_ev"]), ("v", qpos),
                          ("gd", state["store"] if grp4 == 0 else 0)]
                    qb = PB["qbuf"]
                    for j4 in range(4):
                        j = grp4 * 4 + j4
                        P.op("t", lambda e, j=j, j4=j4, qb=qb:
                             e.transpose(pw_tp[:, j4 * 128:(j4 + 1) * 128],
                                         qb[:, j * 128:(j + 1) * 128],
                                         ident[:, :]),
                             inc=(j4 == 3), waits=(wt if j4 == 0 else ()))
                    state["tpos"] = P.n["t"]
                    sv = PB["stripes"].ap().rearrange("p (j f) -> p j f",
                                                      f=512)
                    dst = sv[:, grp4 * 4:grp4 * 4 + 4,
                             stripe_sub * 128:(stripe_sub + 1) * 128]
                    state["bank_ev"] = P.op(
                        "s", lambda e, dst=dst:
                        e.activation(dst, pw_tp[:, :].rearrange(
                            "p (j f) -> p j f", f=128), AFT.Copy),
                        waits=[("t", state["tpos"])])

            def stripes_store(dst_fn):
                spos = P.n["s"]
                state["store"] = P.dma("g", dst_fn, waits=[("s", spos)])

            def mk_units():
                pw_units.append(pwb_open)
                for wsrc, scratch in ((wg_d, qwg_t), (wu_d, qwu_t)):
                    for ot in range(NO):
                        for sub in range(4):
                            rows = (ot * 4 + sub) * 128
                            pw_units.append(
                                lambda src=wsrc[rows:rows + 128, :], sub=sub:
                                pwb_tile(src, sub))
                        pw_units.append(
                            lambda ot=ot, scratch=scratch: stripes_store(
                                lambda e, ot=ot, sp=PB["stripes"],
                                scratch=scratch: e.dma_start(
                                    scratch.ap()[ot].transpose([1, 0, 2]),
                                    sp[:, :].rearrange("p (c f) -> p c f",
                                                       f=512))))
                for hs in range(4):
                    for c in range(4):
                        for sub in range(4):
                            rows = (hs * 4 + sub) * 128
                            pw_units.append(
                                lambda src=wdn_d[rows:rows + 128,
                                                 c * H:(c + 1) * H], sub=sub:
                                pwb_tile(src, sub))

                        def dn_store(hs=hs, c=c):
                            for qq in range(4):
                                dst = qwd_t.ap()[hs * 4 + qq,
                                                 c * NH:(c + 1) * NH, :, :]\
                                    .transpose([1, 0, 2])
                                src = PB["stripes"].ap().rearrange(
                                    "p (j w) -> p j w", w=512)\
                                    [:, :, qq * 128:(qq + 1) * 128]
                                stripes_store(
                                    lambda e, dst=dst, src=src:
                                    e.dma_start(dst, src))
                        pw_units.append(dn_store)
            mk_units()
            if "p1" not in phases:
                while pw_units:
                    pw_units.pop(0)()
                pw_st.close()
                P.barrier()

        stL = contextlib.ExitStack()
        stKV = contextlib.ExitStack()
        if "p0" in phases:
            qT = stL.enter_context(nc.sbuf_tensor("qT", [128, NH * S], BF16,
                                                  side="left"))
            Ksb = stKV.enter_context(nc.sbuf_tensor("Ksb", [128, G * S], BF16,
                                                    side="left"))
            Vtm = stKV.enter_context(nc.sbuf_tensor("Vtm", [128, NT * 512],
                                                    BF16, side="left"))
            stk = contextlib.ExitStack()
            B = {}
            for nm, shp, dt in [("xn32", [128, H], F32),
                                ("xnbf", [128, H], BF16),
                                ("xnT", [128, NH * 512], BF16),
                                ("wqbA", [128, NH * 128], BF16),
                                ("wqbB", [128, NH * 128], BF16)]:
                B[nm] = stk.enter_context(
                    nc.sbuf_tensor(f"p0_{nm}", shp, dt, side="right"))
            pj_ps = [stk.enter_context(nc.psum_tensor(f"pj{i}", [128, 512], F32))
                     for i in range(2)]
            st0 = {"bank_ev": [0, 0], "x_users": (0, 0), "xnT_ev": 0,
                   "qxnT_ev": 0, "pj_ev": [0, 0], "wq_used": [0, 0]}

            def transpose16(src_bf, dst_view_fn, out_dt_note, first_wait):
                last_ev = 0
                for grp4 in range(4):
                    bk = grp4 % 2
                    bank = tp_ps[bk]
                    wt = [("s", st0["bank_ev"][bk])]
                    if grp4 == 0:
                        wt += first_wait
                    for j4 in range(4):
                        j = grp4 * 4 + j4
                        P.op("t", lambda e, j=j, bank=bank, j4=j4, s=src_bf:
                             e.transpose(bank[:, j4 * 128:(j4 + 1) * 128],
                                         s[:, j * 128:(j + 1) * 128],
                                         ident[:, :]),
                             inc=(j4 == 3), waits=(wt if j4 == 0 else ()))
                    tpos = P.n["t"]
                    dst = dst_view_fn(grp4)
                    ev = P.op("s", lambda e, bank=bank, dst=dst:
                              e.activation(dst, bank[:, :].rearrange(
                                  "p (j f) -> p j f", f=128), AFT.Copy),
                              waits=[("t", tpos)])
                    st0["bank_ev"][bk] = ev
                    last_ev = ev
                return last_ev

            for g4 in range(4):
                for tl in range(4):
                    tt = g4 * 4 + tl
                    lx = P.dma("sync", lambda e, tt=tt:
                               e.dma_start(mq.se[:, :],
                                           x_d[tt * 128:(tt + 1) * 128, :]),
                               waits=[("v", st0["x_users"][0]),
                                      ("s", st0["x_users"][1])])
                    P.op("v", lambda e: e.tensor_reduce(
                        sm["sums"][:, :], mq.se[:, :], AXL.X, AOP.add),
                        waits=[("sync", lx)])
                    sqp = P.op("s", lambda e: e.activation(
                        mq.sa[:, :H], mq.se[:, :], AFT.Square,
                        accum_out=sm["sq"][:, :1]), waits=[("sync", lx)])
                    P.op("v", lambda e: e.tensor_scalar(
                        sm["mean"][:, :], sm["sums"][:, :], 1.0 / H, None,
                        AOP.mult))
                    P.op("v", lambda e: e.tensor_tensor(
                        sm["msq"][:, :], sm["mean"][:, :], sm["mean"][:, :],
                        AOP.mult))
                    P.op("v", lambda e: e.tensor_tensor(
                        sm["sq"][:, :], sm["sq"][:, :], sm["mx"][:, :],
                        AOP.add), waits=[("s", sqp)])
                    P.op("v", lambda e: e.scalar_tensor_tensor(
                        sm["var"][:, :], sm["sq"][:, :], 1.0 / H,
                        sm["msq"][:, :], AOP.mult, AOP.subtract))
                    vp = P.op("v", lambda e: e.tensor_scalar(
                        sm["var"][:, :], sm["var"][:, :], 1e-5, None, AOP.add))
                    rp = P.op("s", lambda e: e.activation(
                        sm["var"][:, :], sm["var"][:, :], AFT.Sqrt),
                        waits=[("v", vp)])
                    P.op("v", lambda e: e.reciprocal(
                        sm["rstd"][:, :], sm["var"][:, :]), waits=[("s", rp)])
                    nbp = P.op("v", lambda e: e.scalar_tensor_tensor(
                        sm["negb"][:, :], sm["mean"][:, :], -1.0,
                        sm["rstd"][:, :], AOP.mult, AOP.mult))
                    xnbfp = P.op("s", lambda e: e.activation(
                        B["xnbf"][:, :], mq.se[:, :], AFT.Identity,
                        bias=sm["negb"][:, :1], scale=sm["rstd"][:, :1]),
                        waits=[("v", nbp)])
                    xt_ev = transpose16(
                        B["xnbf"],
                        lambda grp4, tl=tl: B["xnT"].ap().rearrange(
                            "p (j f) -> p j f", f=512)
                        [:, grp4 * 4:(grp4 + 1) * 4, tl * 128:(tl + 1) * 128],
                        BF16, [("s", xnbfp)])
                    st0["xnT_ev"] = xt_ev
                    hp = P.op("s", lambda e: e.activation(
                        B["xn32"][:, :], mq.se[:, :],
                        AFT.Identity, bias=sm["negb"][:, :1],
                        scale=sm["rstd"][:, :1]))
                    qpos = mq.emit(
                        P, B["xn32"][:, :], B["xnbf"][:, :],
                        mfrac=3, max_norm=7.5, act=True, f=H,
                        waits=[("s", hp), ("t", P.n["t"])])
                    st0["x_users"] = (qpos, P.n["s"])
                    st0["qxnT_ev"] = transpose16(
                        B["xnbf"],
                        lambda grp4, tt=tt: qxnT.ap().rearrange(
                            "p (j f) -> p j f", f=S)
                        [:, grp4 * 4:(grp4 + 1) * 4, tt * 128:(tt + 1) * 128],
                        FP8, [("v", qpos)])
                xnT_ready = st0["xnT_ev"]
                for et in range(NH + 8):
                    is_q = et < NH
                    eti = et if is_q else et - NH
                    wbuf = B["wqbA"] if et % 2 == 0 else B["wqbB"]
                    scr = wq_t if is_q else wkv_t
                    lw = P.dma("sync", lambda e, scr=scr, eti=eti, wbuf=wbuf:
                               e.dma_start(
                                   wbuf[:, :].rearrange("p (c f) -> p c f",
                                                        f=128),
                                   scr.ap()[eti].transpose([1, 0, 2])),
                               waits=[("t", st0["wq_used"][et % 2])])
                    bk = et % 2
                    wt = [("sync", lw), ("s", max(xnT_ready, st0["pj_ev"][bk]))]
                    for hc in range(NH):
                        P.op("t", lambda e, hc=hc, wbuf=wbuf, bk=bk:
                             e.matmul(pj_ps[bk][:, :],
                                      wbuf[:, hc * 128:(hc + 1) * 128],
                                      B["xnT"][:, hc * 512:(hc + 1) * 512],
                                      start=(hc == 0), stop=(hc == NH - 1)),
                             inc=(hc == NH - 1), waits=(wt if hc == 0 else ()))
                    st0["wq_used"][et % 2] = P.n["t"]
                    mmpos = P.n["t"]
                    if is_q:
                        dst = qT[:, eti * S + g4 * 512: eti * S + g4 * 512 + 512]
                        st0["pj_ev"][bk] = P.op(
                            "s", lambda e, dst=dst, bk=bk:
                            e.activation(dst, pj_ps[bk][:, :], AFT.Copy),
                            waits=[("t", mmpos)])
                    elif eti < 4:
                        dst = Ksb[:, eti * S + g4 * 512: eti * S + g4 * 512 + 512]
                        st0["pj_ev"][bk] = P.op(
                            "s", lambda e, dst=dst, bk=bk:
                            e.activation(dst, pj_ps[bk][:, :], AFT.Copy),
                            waits=[("t", mmpos)])
                    else:
                        g = eti - 4
                        vev = P.op("s", lambda e, bk=bk: e.activation(
                            B["xnbf"][:, :512], pj_ps[bk][:, :], AFT.Copy),
                            waits=[("t", mmpos)])
                        st0["pj_ev"][bk] = vev
                        bank = tp_ps[0]
                        wt2 = [("s", max(st0["bank_ev"][0], vev))]
                        for i in range(4):
                            P.op("t", lambda e, i=i, bank=bank:
                                 e.transpose(bank[:, i * 128:(i + 1) * 128],
                                             B["xnbf"][:, i * 128:(i + 1) * 128],
                                             ident[:, :]),
                                 inc=(i == 3), waits=(wt2 if i == 0 else ()))
                        tpos = P.n["t"]
                        dst = Vtm.ap().rearrange("p (tb f) -> p tb f", f=512)\
                            [:, g4 * 4:(g4 + 1) * 4, g * 128:(g + 1) * 128]
                        st0["bank_ev"][0] = P.op(
                            "s", lambda e, bank=bank, dst=dst:
                            e.activation(dst, bank[:, :].rearrange(
                                "p (j f) -> p j f", f=128), AFT.Copy),
                            waits=[("t", tpos)])
            stk.close()
            P.barrier()
            if "qxnT" in dump:
                dq = nc.dram_tensor("dbg_qxnT", [128, NH * S], FP8,
                                    kind="ExternalOutput")
                P.dma("g", lambda e: e.dma_start(dq[:, :], qxnT[:, :]))
                P.barrier()

        if "p1" in phases:
            stk = contextlib.ExitStack()
            Pbuf = stk.enter_context(nc.sbuf_tensor("Pbuf", [128, S], BF16,
                                                    side="right"))
            PT = stk.enter_context(nc.sbuf_tensor("PT", [128, S], BF16,
                                                  side="right"))
            sc_ps = stk.enter_context(nc.psum_tensor("sc_ps", [128, S], F32))
            cx_ps = stk.enter_context(nc.psum_tensor("cx_ps", [128, 512], F32))
            s1 = {"cons_v": 0, "cons_s": 0, "pt_ev": 0, "cx_ev": 0,
                  "pt_used": 0}
            for h in range(HEADS):
                g = h // (HEADS // G)
                for tq in range(NT):
                    if pw_units:
                        pw_units.pop(0)()
                    ncov = tq + 1
                    W = ncov * 128
                    nkb = (W + 511) // 512
                    wt = [("v", s1["cons_v"]), ("s", s1["cons_s"])]
                    for kb in range(nkb):
                        n = min(512, W - kb * 512)
                        P.op("t", lambda e, kb=kb, n=n, h=h, tq=tq, g=g:
                             e.matmul(sc_ps[:, kb * 512: kb * 512 + n],
                                      qT[:, h * S + tq * 128:
                                         h * S + tq * 128 + 128],
                                      Ksb[:, g * S + kb * 512:
                                          g * S + kb * 512 + n],
                                      start=True, stop=True),
                             inc=(kb == nkb - 1), waits=(wt if kb == 0 else ()))
                    qkpos = P.n["t"]
                    mp = P.op("v", lambda e, tq=tq: e.tensor_tensor(
                        sc_ps[:, tq * 128:(tq + 1) * 128],
                        sc_ps[:, tq * 128:(tq + 1) * 128], mask[:, :],
                        AOP.add), waits=[("t", qkpos)])
                    P.op("v", lambda e, W=W: e.tensor_reduce(
                        sm["mx"][:, :], sc_ps[:, :W], AXL.X, AOP.max))
                    nbp = P.op("v", lambda e: e.tensor_scalar(
                        sm["nb"][:, :], sm["mx"][:, :], -SCALE, None,
                        AOP.mult))
                    s1["cons_v"] = nbp
                    ep = P.op("s", lambda e, W=W: e.activation(
                        Pbuf[:, :W], sc_ps[:, :W], AFT.Exp,
                        bias=sm["nb"][:, :1], scale=SCALE,
                        accum_out=sm["sume"][:, :1]), waits=[("v", nbp)])
                    s1["cons_s"] = ep
                    rp1 = P.op("v", lambda e: e.reciprocal(
                        rsum[:, 0:1], sm["sume"][:, :]),
                        waits=[("s", ep)])
                    ep = P.op("s", lambda e, W=W: e.activation(
                        Pbuf[:, :W], Pbuf[:, :W], AFT.Identity,
                        scale=rsum[:, 0:1]), waits=[("v", rp1)])
                    s1["cons_s"] = ep
                    ptev = 0
                    for g4 in range((ncov + 3) // 4):
                        nj = min(4, ncov - g4 * 4)
                        bk = g4 % 2
                        bank = tp_ps[bk]
                        wt2 = [("s", max(ep, s1.get(f"bkev{bk}", 0)))]
                        if g4 == 0:
                            wt2.append(("t", s1["pt_used"]))
                        for j in range(nj):
                            kb = g4 * 4 + j
                            P.op("t", lambda e, kb=kb, j=j, bank=bank:
                                 e.transpose(bank[:, j * 128:(j + 1) * 128],
                                             Pbuf[:, kb * 128:(kb + 1) * 128],
                                             ident[:, :]),
                                 inc=(j == nj - 1),
                                 waits=(wt2 if j == 0 else ()))
                        tpos = P.n["t"]
                        ptev = P.op("s", lambda e, bank=bank, g4=g4, nj=nj:
                                    e.activation(
                                        PT[:, g4 * 512: g4 * 512 + nj * 128],
                                        bank[:, :nj * 128], AFT.Copy),
                                    waits=[("t", tpos)])
                        s1[f"bkev{bk}"] = ptev
                    wt3 = [("s", ptev), ("s", s1["cx_ev"])]
                    for kb in range(ncov):
                        P.op("t", lambda e, kb=kb, g=g:
                             e.matmul(cx_ps[:, :128],
                                      Vtm[:, kb * 512 + g * 128:
                                          kb * 512 + g * 128 + 128],
                                      PT[:, kb * 128:(kb + 1) * 128],
                                      start=(kb == 0), stop=(kb == ncov - 1)),
                             inc=(kb == ncov - 1),
                             waits=(wt3 if kb == 0 else ()))
                    pvpos = P.n["t"]
                    s1["pt_used"] = pvpos
                    s1["cx_ev"] = P.op(
                        "s", lambda e, h=h, tq=tq: e.activation(
                            qT[:, h * S + tq * 128: h * S + tq * 128 + 128],
                            cx_ps[:, :128], AFT.Copy), waits=[("t", pvpos)])
            while pw_units:
                pw_units.pop(0)()
            pw_st.close()
            stk.close()
            P.barrier()

        stKV.close()
        mq_st.close()

        if "p1b" in phases:
            stk = contextlib.ExitStack()
            wdsb = stk.enter_context(nc.sbuf_tensor("wdsb", [128, NH * H],
                                                    BF16, side="right"))
            xrA = stk.enter_context(nc.sbuf_tensor("xrA", [128, H], F32,
                                                   side="right"))
            xrB = stk.enter_context(nc.sbuf_tensor("xrB", [128, H], F32,
                                                   side="right"))
            obuf = stk.enter_context(nc.sbuf_tensor("obuf", [128, H], F32,
                                                    side="right"))
            at_ps = stk.enter_context(nc.psum_tensor("at_ps", [128, H], F32))
            lw = 0
            for r in range(NH):
                lw = P.dma("sync", lambda e, r=r: e.dma_start(
                    wdsb[:, r * H:(r + 1) * H],
                    wd_b[r * 128:(r + 1) * 128, :]))
            sb1 = {"stt": 0, "store": 0, "lx": {}, "stt_hist": {}}

            def load_x(tq):
                if tq >= NT:
                    return
                xr = xrA if tq % 2 == 0 else xrB
                sb1["lx"][tq] = P.dma(
                    "sync", lambda e, tq=tq, xr=xr: e.dma_start(
                        xr[:, :], x_d[tq * 128:(tq + 1) * 128, :]),
                    waits=[("v", sb1["stt_hist"].get(tq - 2, 0))])
            load_x(0)
            load_x(1)
            for tq in range(NT):
                xr = xrA if tq % 2 == 0 else xrB
                wt = [("sync", lw), ("v", sb1["stt"])]
                for ec in range(NH):
                    for hq in range(4):
                        P.op("t", lambda e, ec=ec, hq=hq, tq=tq:
                             e.matmul(at_ps[:, hq * 512:(hq + 1) * 512],
                                      qT[:, ec * S + tq * 128:
                                         ec * S + tq * 128 + 128],
                                      wdsb[:, ec * H + hq * 512:
                                           ec * H + hq * 512 + 512],
                                      start=(ec == 0), stop=(ec == NH - 1)),
                             inc=(ec == NH - 1 and hq == 3),
                             waits=(wt if ec == 0 and hq == 0 else ()))
                mmpos = P.n["t"]
                sb1["stt"] = P.op("v", lambda e, tq=tq, xr=xr:
                                  e.tensor_tensor(
                                      obuf[:, :], at_ps[:, :], xr[:, :],
                                      AOP.add),
                                  waits=[("t", mmpos),
                                         ("sync", sb1["lx"][tq]),
                                         ("gd", sb1["store"])])
                sb1["stt_hist"][tq] = sb1["stt"]
                sb1["store"] = P.dma("g", lambda e, tq=tq: e.dma_start(
                    out_d[tq * 128:(tq + 1) * 128, :], obuf[:, :]),
                    waits=[("v", sb1["stt"])])
                load_x(tq + 2)
            stk.close()
            P.barrier()
        stL.close()

        if "p2" in phases:
            stk = contextlib.ExitStack()
            sbR = lambda n, shp, dt: stk.enter_context(
                nc.sbuf_tensor(n, shp, dt, side="right"))
            oacc = stL2 = stk.enter_context(nc.sbuf_tensor(
                "oacc", [128, 4 * H], F32, side="left"))
            qintT = stk.enter_context(nc.sbuf_tensor(
                "qintT", [128, NOC * 512], FP8, side="left"))
            wgA = sbR("wgA", [128, NH * 512], FP8)
            wgB = sbR("wgB", [128, NH * 512], FP8)
            wuA = sbR("wuA", [128, NH * 512], FP8)
            wuB = sbR("wuB", [128, NH * 512], FP8)
            wdA = sbR("wdA", [128, NOC * 128], FP8)
            wdB = sbR("wdB", [128, NOC * 128], FP8)
            sg = sbR("sg", [128, 512], F32)
            ibuf = sbR("ibuf", [128, 512], F32)
            qibA = sbR("qibA", [128, 512], BF16)
            qibB = sbR("qibB", [128, 512], BF16)
            fTA = sbR("fTA", [128, 512], BF16)
            fTB = sbR("fTB", [128, 512], BF16)
            mq2_st = contextlib.ExitStack()
            mq2 = MXQuant(mq2_st, nc, "mq2", 128, 512, side="right")
            g_ps = stk.enter_context(nc.psum_tensor("g_ps", [128, 512], F32))
            u_ps = stk.enter_context(nc.psum_tensor("u_ps", [128, 512], F32))
            d_ps = stk.enter_context(nc.psum_tensor("d_ps", [128, 512], F32))
            tq_ps = stk.enter_context(nc.psum_tensor("tq_ps", [128, 512], BF16))
            tr_ps = stk.enter_context(nc.psum_tensor("tr_ps", [128, 512], BF16))
            s2 = {"oacc_v": 0, "store": 0, "lg_used": [0, 0], "lu_used": [0, 0],
                  "ld_used": [0, 0], "silu": 0, "mul": 0, "qev": 0, "qtp": 0,
                  "dev": 0, "ftp": 0, "addv": 0}
            for qtr in range(4):
                lo = 0
                for i in range(4):
                    lo = P.dma("sync", lambda e, qtr=qtr, i=i: e.dma_start(
                        oacc[:, i * H:(i + 1) * H],
                        out_d[(qtr * 4 + i) * 128:(qtr * 4 + i + 1) * 128, :]),
                        waits=[("v", s2["addv"]), ("gd", s2["store"])])
                for ot in range(NO):
                    bk = ot % 2
                    wg = wgA if bk == 0 else wgB
                    wu = wuA if bk == 0 else wuB
                    lg = P.dma("sync", lambda e, ot=ot, wg=wg: e.dma_start(
                        wg[:, :].rearrange("p (c f) -> p c f", f=512),
                        qwg_t.ap()[ot].transpose([1, 0, 2])),
                        waits=[("t", s2["lg_used"][bk])])
                    lu = P.dma("sync", lambda e, ot=ot, wu=wu: e.dma_start(
                        wu[:, :].rearrange("p (c f) -> p c f", f=512),
                        qwu_t.ap()[ot].transpose([1, 0, 2])),
                        waits=[("t", s2["lu_used"][bk])])
                    for tl in range(4):
                        gt = qtr * 4 + tl
                        wt = [("sync", lg), ("s", s2["silu"])]
                        for hc in range(NH):
                            P.op("t", lambda e, hc=hc, gt=gt, wg=wg:
                                 e.matmul(g_ps[:, :],
                                          qxnT[:, hc * S + gt * 128:
                                               hc * S + gt * 128 + 128],
                                          wg[:, hc * 512:(hc + 1) * 512],
                                          start=(hc == 0), stop=(hc == NH - 1)),
                                 inc=(hc == NH - 1),
                                 waits=(wt if hc == 0 else ()))
                        gpos = P.n["t"]
                        s2["lg_used"][bk] = gpos
                        wt = [("sync", lu), ("v", s2["mul"])]
                        for hc in range(NH):
                            P.op("t", lambda e, hc=hc, gt=gt, wu=wu:
                                 e.matmul(u_ps[:, :],
                                          qxnT[:, hc * S + gt * 128:
                                               hc * S + gt * 128 + 128],
                                          wu[:, hc * 512:(hc + 1) * 512],
                                          start=(hc == 0), stop=(hc == NH - 1)),
                                 inc=(hc == NH - 1),
                                 waits=(wt if hc == 0 else ()))
                        upos = P.n["t"]
                        s2["lu_used"][bk] = upos
                        s2["silu"] = P.op("s", lambda e: e.activation(
                            sg[:, :], g_ps[:, :], AFT.Silu),
                            waits=[("t", gpos)])
                        s2["mul"] = P.op("v", lambda e: e.tensor_tensor(
                            ibuf[:, :], sg[:, :], u_ps[:, :], AOP.mult),
                            waits=[("s", s2["silu"]), ("t", upos)])
                        qib = qibA if (ot * 4 + tl) % 2 == 0 else qibB
                        qpos = mq2.emit(P, ibuf[:, :], qib[:, :], mfrac=3,
                                        max_norm=7.5, act=True, f=512,
                                        waits=[("t", s2["qtp"])])

                        def qtp_flush(qib=qib, qpos=qpos, ot=ot, tl=tl):
                            wt2 = [("v", qpos), ("s", s2["qev"])]
                            for j in range(4):
                                P.op("t", lambda e, j=j, qib=qib:
                                     e.transpose(
                                         tq_ps[:, j * 128:(j + 1) * 128],
                                         qib[:, j * 128:(j + 1) * 128],
                                         ident[:, :]),
                                     inc=(j == 3), waits=(wt2 if j == 0 else ()))
                            s2["qtp"] = P.n["t"]
                            dst = qintT.ap().rearrange(
                                "p (c f) -> p c f", f=512)\
                                [:, ot * 4:(ot + 1) * 4,
                                 tl * 128:(tl + 1) * 128]
                            s2["qev"] = P.op(
                                "s", lambda e, dst=dst: e.activation(
                                    dst, tq_ps[:, :].rearrange(
                                        "p (j f) -> p j f", f=128), AFT.Copy),
                                waits=[("t", s2["qtp"])])
                        if s2.get("pend") is not None:
                            s2["pend"]()
                        s2["pend"] = qtp_flush
                if s2.get("pend") is not None:
                    s2["pend"]()
                    s2["pend"] = None
                qint_ready = s2["qev"]
                if "qintT" in dump and qtr == 0:
                    dqi = nc.dram_tensor("dbg_qintT", [128, NOC * 512], FP8,
                                         kind="ExternalOutput")
                    P.dma("g", lambda e: e.dma_start(dqi[:, :], qintT[:, :]),
                          waits=[("s", qint_ready)])
                for ht in range(NH):
                    bk = ht % 2
                    wd2 = wdA if bk == 0 else wdB
                    ld = P.dma("sync", lambda e, ht=ht, wd2=wd2: e.dma_start(
                        wd2[:, :].rearrange("p (c f) -> p c f", f=128),
                        qwd_t.ap()[ht].transpose([1, 0, 2])),
                        waits=[("t", s2["ld_used"][bk])])
                    wt = [("sync", ld), ("s", max(qint_ready, s2["dev"]))]
                    for oc in range(NOC):
                        P.op("t", lambda e, oc=oc, wd2=wd2:
                             e.matmul(d_ps[:, :],
                                      wd2[:, oc * 128:(oc + 1) * 128],
                                      qintT[:, oc * 512:(oc + 1) * 512],
                                      start=(oc == 0), stop=(oc == NOC - 1)),
                             inc=(oc == NOC - 1), waits=(wt if oc == 0 else ()))
                    dpos = P.n["t"]
                    s2["ld_used"][bk] = dpos
                    fT = fTA if ht % 2 == 0 else fTB
                    s2["dev"] = P.op("s", lambda e, fT=fT: e.activation(
                        fT[:, :], d_ps[:, :], AFT.Copy), waits=[("t", dpos)])

                    def ftp_flush(ht=ht, fT=fT, dev=s2["dev"], lo=lo):
                        wt2 = [("s", dev), ("v", s2["addv"])]
                        for j in range(4):
                            P.op("t", lambda e, j=j, fT=fT:
                                 e.transpose(tr_ps[:, j * 128:(j + 1) * 128],
                                             fT[:, j * 128:(j + 1) * 128],
                                             ident[:, :]),
                                 inc=(j == 3), waits=(wt2 if j == 0 else ()))
                        ftp = P.n["t"]
                        dst = oacc.ap().rearrange("p (i f) -> p i f", f=H)\
                            [:, :, ht * 128:(ht + 1) * 128]
                        s2["addv"] = P.op(
                            "v", lambda e, dst=dst: e.tensor_tensor(
                                dst, tr_ps[:, :].rearrange(
                                    "p (j f) -> p j f", f=128), dst, AOP.add),
                            waits=[("t", ftp), ("sync", lo)])
                    if s2.get("dpend") is not None:
                        s2["dpend"]()
                    s2["dpend"] = ftp_flush
                if s2.get("dpend") is not None:
                    s2["dpend"]()
                    s2["dpend"] = None
                for i in range(4):
                    s2["store"] = P.dma("g", lambda e, qtr=qtr, i=i:
                                        e.dma_start(
                                            out_d[(qtr * 4 + i) * 128:
                                                  (qtr * 4 + i + 1) * 128, :],
                                            oacc[:, i * H:(i + 1) * H]),
                                        waits=[("v", s2["addv"])])
            mq2_st.close()
            stk.close()

        P.wait("g", "gd", P.nd["gd"])
        P.barrier()

        with nc.Block() as block:
            @block.sync
            def _(e):
                for st_ in P.steps["sync"]:
                    st_(e, sems)

            @block.vector
            def _(e):
                for st_ in P.steps["v"]:
                    st_(e, sems)

            @block.scalar
            def _(e):
                for st_ in P.steps["s"]:
                    st_(e, sems)

            @block.tensor
            def _(e):
                for st_ in P.steps["t"]:
                    st_(e, sems)

            @block.gpsimd
            def _(e):
                for st_ in P.steps["g"]:
                    st_(e, sems)

    return nc


def make_consts():
    ident = np.eye(128, dtype=ml_dtypes.bfloat16)
    mask = np.triu(np.full((128, 128), -1e30, dtype=np.float32), k=1)
    return ident, mask


def kernel(L0_attn_x, L0_attn_w_q, L0_attn_w_kv, L0_attn_w_dense,
           L0_ffn_w_gate, L0_ffn_w_up, L0_ffn_w_down):
    ident, mask = make_consts()
    xs = np.ascontiguousarray(np.asarray(L0_attn_x, dtype=np.float32))
    base = {
        "wq": np.ascontiguousarray(L0_attn_w_q, dtype=np.float32),
        "wkv": np.ascontiguousarray(L0_attn_w_kv, dtype=np.float32),
        "wd": np.ascontiguousarray(L0_attn_w_dense, dtype=np.float32),
        "wg": np.ascontiguousarray(L0_ffn_w_gate, dtype=np.float32),
        "wu": np.ascontiguousarray(L0_ffn_w_up, dtype=np.float32),
        "wdn": np.ascontiguousarray(L0_ffn_w_down, dtype=np.float32),
        "ident": ident, "mask": mask,
    }
    nc = build_nc()
    in_maps = [dict(base, x=xs[c]) for c in range(8)]
    res = run_bass_kernel_spmd(nc, in_maps, core_ids=list(range(8)))
    return np.stack([r["out"] for r in res.results], axis=0)
